# version 26 (final)
# baseline (speedup 1.0000x reference)
import math
import numpy as np
import ml_dtypes
import concourse.bass as bass
import concourse.mybir as mybir
from concourse.bass_utils import run_bass_kernel_spmd
from concourse.aot_env import allow_ici_all_to_all

F32 = mybir.dt.float32
BF16 = mybir.dt.bfloat16
AF = mybir.ActivationFunctionType
ALU = mybir.AluOpType
AX = mybir.AxisListType
BF = ml_dtypes.bfloat16

S, DM, HD, NH = 4096, 4096, 128, 32
EPS = 1e-5
QK_SCALE = 1.0 / math.sqrt(HD)
NQ_SLOT = [64, 128, 256, 512, 512, 512, 512, 512]
DELTA_CAP = 25.0
ISC32 = 1.0 / math.sqrt(32.0)
SLOPES = [2.0 ** (-0.25 * (i + 1)) for i in range(NH)]

MASK_SPECS = [(512, 0), (512, 1), (512, 2), (512, 3), (256, 0), (256, 1),
              (128, 0), (64, 0), (64, 64)]
MASK_OFF = {}
_o = 0
for _nq, _a in MASK_SPECS:
    MASK_OFF[(_nq, _a)] = _o
    _o += _nq
MASK_W = _o

BCOLS = {}
for _s in range(3, 8):
    for _g in range(8):
        for _kb in range(4 * (_g + 1)):
            _d = (_g + 1) * 512 - 128 * _kb
            if (_s, _d) not in BCOLS:
                BCOLS[(_s, _d)] = len(BCOLS)
for _s in range(3):
    for _kb in range(32):
        BCOLS[(_s, "k", _kb)] = len(BCOLS)
NBC = len(BCOLS)

ENGS = ("sp", "pool", "act", "dve", "pe")
_BLK = {"sp": "sync", "pool": "gpsimd", "act": "scalar", "dve": "vector",
        "pe": "tensor"}


N_LANES = 4
DBG_STATIC_RECV = False


class Rec:
    def __init__(self, nc):
        self.nc = nc
        names = ["pe", "act", "dve", "pool", "cc"]
        for q in ("d_sp", "d_act", "d_pool"):
            for l in range(N_LANES):
                names.append("%s%d" % (q, l))
        self.sems = {n: nc.alloc_semaphore("s_" + n) for n in names}
        self.cnt = {n: 0 for n in names}
        self.waited = {e: {n: 0 for n in names} for e in ENGS}
        self.ops = {e: [] for e in ENGS}
        self.qidx = {"d_sp": 0, "d_act": 0, "d_pool": 0}
        self.rtv = {}

    def emit(self, eng, fn, wait=(), inc=None, step=1):
        ws = []
        for t in wait:
            if t is None:
                continue
            s, v = t
            if v > self.waited[eng][s]:
                ws.append((self.sems[s], v))
                self.waited[eng][s] = v
        tok = None
        semh = None
        if inc is not None:
            self.cnt[inc] += step
            tok = (inc, self.cnt[inc])
            semh = self.sems[inc]

        def closure(e, ws=tuple(ws), fn=fn, semh=semh, step=step):
            for sh, v in ws:
                e.wait_ge(sh, v)
            if fn is not None:
                ins = fn(e)
                if semh is not None:
                    ins.then_inc(semh, step)
        self.ops[eng].append(closure)
        return tok

    def flush(self, name):
        with self.nc.Block(name) as blk:
            for e in ENGS:
                if self.ops[e]:
                    def section(eng, lst=tuple(self.ops[e])):
                        for f in lst:
                            f(eng)
                    getattr(blk, _BLK[e])(section)
        self.ops = {e: [] for e in ENGS}
        for e in ENGS:
            for s in ("pe", "act", "dve", "pool"):
                self.waited[e][s] = max(self.waited[e][s], self.cnt[s])


def build_program(stop_after=None):
    import contextlib
    nc = bass.Bass("TRN2", target_bir_lowering=False)

    x_in = nc.dram_tensor("x", [S, DM], F32, kind="ExternalInput").ap()
    xown_in = nc.dram_tensor("x_own", [1024, DM], F32, kind="ExternalInput").ap()
    wq8_in = nc.dram_tensor("wq8", [DM, 1024], F32, kind="ExternalInput").ap()
    wkv_in = nc.dram_tensor("wkv", [DM, 256], F32, kind="ExternalInput").ap()
    wo_in = nc.dram_tensor("wo", [DM, DM], F32, kind="ExternalInput").ap()
    wt_in = nc.dram_tensor("wt", [DM, DM], F32, kind="ExternalInput").ap()
    brow_in = nc.dram_tensor("brow", [1, DM], F32, kind="ExternalInput").ap()
    wqka_in = nc.dram_tensor("wqka", [DM, 640], F32, kind="ExternalInput").ap()
    bm_in = nc.dram_tensor("bm", [128, DM], F32, kind="ExternalInput").ap()
    prep_in = nc.dram_tensor("prep", [8, 128], F32, kind="ExternalInput").ap()
    bdiag_in = nc.dram_tensor("bdiag", [128, 16], F32, kind="ExternalInput").ap()
    masks_in = nc.dram_tensor("masks", [128, 1024], F32, kind="ExternalInput").ap()
    btab_in = nc.dram_tensor("btab", [128, NBC], F32,
                             kind="ExternalInput").ap()
    offr3_in = nc.dram_tensor("offr3", [72, 512], BF16,
                              kind="ExternalInput").ap()
    ident_in = nc.dram_tensor("ident", [128, 128], F32, kind="ExternalInput").ap()
    out_dram = nc.dram_tensor("out", [1024, DM], F32, kind="ExternalOutput").ap()

    att_send = [nc.dram_tensor("att_send%d" % i, [8, 512, 1024], BF16,
                                kind="Internal").ap() for i in range(2)]
    att_recv = [nc.dram_tensor("att_recv%d" % i, [8, 512, 1024], BF16,
                               kind="Internal").ap() for i in range(2)]

    A = nc.alloc_sbuf_tensor
    big_a = A("big_a", [128, 32768], BF16)
    big_b = A("big_b", [128, 32768], BF16)
    stream_w = A("stream_w", [128, 8192], BF16)
    strip0 = A("strip0", [128, 4096], F32)
    scratch = A("scratch", [128, 4352], F32)
    scratch_bf = scratch.bitcast(BF16)

    class _OffView:
        def __init__(self, h, off, width):
            self.h, self.off, self.width = h, off, width

        def __getitem__(self, idx):
            rows, cols = idx
            a = cols.start if cols.start is not None else 0
            b = cols.stop if cols.stop is not None else self.width
            return self.h[rows, self.off + a:self.off + b]

    strip = [strip0, _OffView(scratch, 0, 4096)]
    probs = _OffView(scratch_bf, 0, 2048)
    rb_sb = _OffView(scratch, 1024, 1024)
    rrow = _OffView(scratch, 2048, 1024)
    hcol = _OffView(scratch_bf, 6144, 1024)
    hw_sb = _OffView(scratch_bf, 7168, 1024)
    bm_tile = _OffView(scratch_bf, 8192, 512)
    xnstage = A("xnstage", [128, 4096], BF16)
    kT = A("kT", [128, S], BF16)
    vtm = A("vtm", [128, S], BF16)
    strip_bf = strip0.bitcast(BF16)
    identb = A("identb", [128, 128], BF16)
    prep_sb = A("prep_sb", [8, 128], F32)
    bdiag_sb = A("bdiag_sb", [128, 16], F32)
    brow_sb = A("brow_sb", [1, 512], BF16)
    onesc_bf = A("onesc_bf", [128, 128], BF16)
    ones3 = A("ones3", [3, 128], BF16)
    offr3_sb = A("offr3_sb", [3, 1024], BF16)
    btab_sb = A("btab_sb", [128, NBC], F32)
    onesr_bf = A("onesr_bf", [1, 128], BF16)
    onesr_f = A("onesr_f", [1, 128], F32)
    ones8_f = A("ones8_f", [8, 1], F32)
    sexp_sb = rb_sb
    rsum_sb = rrow
    out_sb = rb_sb
    st_bn = A("st_bn", [128, 96], F32)
    st_mv = A("st_mv", [128, 4], F32)
    st_ln = A("st_ln", [128, 2], F32)
    st_r = A("st_r", [128, 2], F32)
    st_mr = A("st_mr", [128, 2], F32)
    st_eps = A("st_eps", [128, 1], F32)

    rec = Rec(nc)
    E = rec.emit
    DQ = {"sp": "d_sp", "act": "d_act", "pool": "d_pool"}

    def _lane(q):
        base = DQ[q]
        lane = "%s%d" % (base, rec.qidx[base] % N_LANES)
        rec.qidx[base] += 1
        pre = (lane, rec.cnt[lane])
        return lane, pre

    def dma(q, out, in_, wait=()):
        lane, pre = _lane(q)
        return E(q, lambda e, o=out, i=in_: e.dma_start(out=o, in_=i),
                 wait=list(wait) + [pre], inc=lane, step=16)

    def dma_t(q, out, in_, wait=()):
        lane, pre = _lane(q)
        return E(q, lambda e, o=out, i=in_: e.dma_start(out=o, in_=i,
                 transpose=True), wait=list(wait) + [pre], inc=lane, step=16)

    def mm(out, lhsT, rhs, start, stop, wait=(), inc=None):
        return E("pe", lambda e, o=out, l=lhsT, r=rhs, st=start, sp=stop:
                 e.matmul(o, l, r, start=st, stop=sp, skip_group_check=True),
                 wait=wait, inc=inc)

    t_ident = dma("pool", identb[:, :], ident_in)
    dma("sp", prep_sb[:, :], prep_in)
    t_btab = dma("sp", btab_sb[:, :], btab_in)
    dma("sp", bdiag_sb[:, :], bdiag_in)
    E("pool", lambda e: e.memset(st_eps[:, :], EPS), inc="pool")
    E("pool", lambda e: e.memset(onesc_bf[:, :], 1.0), inc="pool")
    E("pool", lambda e: e.memset(ones3[:, :], 1.0), inc="pool")
    E("pool", lambda e: e.memset(onesr_bf[:, :], 1.0), inc="pool")
    E("pool", lambda e: e.memset(onesr_f[:, :], 1.0), inc="pool")
    E("pool", lambda e: e.memset(ones8_f[:, :], 1.0), inc="pool")
    rec.flush("init")
    if stop_after == "init":
        return nc

    ps = contextlib.ExitStack()
    tps = [ps.enter_context(nc.psum_tensor("tps%d" % i, [128, 512], BF16))
           for i in range(2)]
    kacc = ps.enter_context(nc.psum_tensor("kacc", [128, 512], F32))
    vps = ps.enter_context(nc.psum_tensor("vps", [128, 512], F32))
    qacc = [ps.enter_context(nc.psum_tensor("qacc%d" % i, [128, 512], F32))
            for i in range(4)]

    def bb(h, ic):
        return big_b[:, h * 16384 + ic * 512:h * 16384 + (ic + 1) * 512]

    def bb4(h, ic0, toff):
        return (big_b[:, h * 16384:(h + 1) * 16384]
                .rearrange("p (ic t) -> p ic t", t=512)
                [:, ic0:ic0 + 4, toff:toff + 128])

    strip_free = [(), ()]
    st_free = [(), ()]
    mv_free = [(), ()]
    r_free = [None, None]
    xs_free = [None] * 8
    tps_free = [None, None]
    xscnt = [0]
    bb_free = [None]
    halfev = [None, None]
    k_free = [None]
    v_free = [None]
    w_free = [None, None]
    q_evac = [None, None, None, None]
    q_last = [None]
    wkv_sb = stream_w[:, :].rearrange("p (ic n) -> p ic n", n=256)
    wq_t = [stream_w[:, 0:4096].rearrange("p (ic c) -> p ic c", c=128),
            stream_w[:, 4096:8192].rearrange("p (ic c) -> p ic c", c=128)]

    def emit_strip(gi, src_rows):
        par = gi % 2
        rs = gi % 8
        h = rs // 4
        toff = (rs % 4) * 128
        tlA = dma("sp", strip[par][:, 0:2048], src_rows[:, 0:2048],
                  wait=strip_free[par])
        tlB = dma("act", strip[par][:, 2048:4096], src_rows[:, 2048:4096],
                  wait=strip_free[par])
        tb0 = None
        for c in range(8):
            w = [tlA if c < 4 else tlB]
            if c == 0:
                w += list(st_free[par])
            tb0 = E("dve", lambda e, o=st_bn[:, par * 48 + c * 6:
                                             par * 48 + (c + 1) * 6],
                    s=strip[par][:, c * 512:(c + 1) * 512]:
                    e.bn_stats(o, s), wait=w, inc="dve")
        t_ag = E("dve", lambda e, o=st_mv[:, 2 * par:2 * par + 2],
                 i_=st_bn[:, par * 48:par * 48 + 48]:
                 e.bn_aggr(o, i_), wait=[tb0] + list(mv_free[par]),
                 inc="dve")
        st_free[par] = (t_ag,)
        t_ln = E("act", lambda e, i_=st_mv[:, 2 * par + 1:2 * par + 2],
                 o=st_ln[:, par:par + 1]:
                 e.activation(out=o, in_=i_, func=AF.Ln,
                              bias=st_eps[:, 0:1]),
                 wait=[t_ag], inc="act")
        t_r = E("act", lambda e, i_=st_ln[:, par:par + 1],
                o=st_r[:, par:par + 1]:
                e.activation(out=o, in_=i_, func=AF.Exp, scale=-0.5),
                wait=[t_ln, r_free[par]], inc="act")
        t_mr = E("dve", lambda e, m=st_mv[:, 2 * par:2 * par + 1],
                 r_=st_r[:, par:par + 1], o=st_mr[:, par:par + 1]:
                 e.scalar_tensor_tensor(o, m, -1.0, r_, ALU.mult, ALU.mult),
                 wait=[t_r], inc="dve")
        mv_free[par] = (t_mr, t_ln)
        last_nm = None
        last_ev = None
        for blk in range(8):
            xs = xscnt[0] % 8
            xscnt[0] += 1
            t_nm = E("pool", lambda e,
                     o=xnstage[:, xs * 512:(xs + 1) * 512],
                     s=strip[par][:, blk * 512:(blk + 1) * 512],
                     r_=st_r[:, par:par + 1], b=st_mr[:, par:par + 1]:
                     e.tensor_scalar(o, s, r_, b, ALU.mult, ALU.add),
                     wait=[t_mr, tlA if blk < 4 else tlB, xs_free[xs]],
                     inc="pool")
            last_nm = t_nm
            tp = blk % 2
            ttr = None
            for i in range(4):
                wv = [t_nm, tps_free[tp], t_ident] if i == 0 else []
                ttr = E("pe", lambda e, o=tps[tp][:, i * 128:(i + 1) * 128],
                        i_=xnstage[:, xs * 512 + i * 128:
                                   xs * 512 + (i + 1) * 128]:
                        e.transpose(o, i_, identb[:, :]),
                        wait=wv, inc="pe" if i == 3 else None)
            xs_free[xs] = ttr
            t_ev = E("dve", lambda e,
                     o=bb4(h, blk * 4, toff),
                     i_=tps[tp][:, 0:512].rearrange("p (i t) -> p i t",
                                                    t=128):
                     e.tensor_copy(o, i_),
                     wait=[ttr, bb_free[0]], inc="dve")
            tps_free[tp] = t_ev
            last_ev = t_ev
        strip_free[par] = (last_nm, tb0)
        r_free[par] = last_nm
        return last_ev

    def emit_kv(tsp, h, t_wkv):
        tk = None
        for ic in range(32):
            wv = [t_wkv, halfev[h], k_free[0]] if ic == 0 else []
            tk = mm(kacc[:, :], wkv_sb[:, ic, 0:128], bb(h, ic),
                    ic == 0, ic == 31, wait=wv,
                    inc="pe" if ic == 31 else None)
        k_free[0] = E("dve", lambda e,
                      o=kT[:, (tsp * 2 + h) * 512:(tsp * 2 + h + 1) * 512],
                      src=kacc[:, :]:
                      e.tensor_copy(o, src), wait=[tk], inc="dve")
        tv = None
        for ic in range(32):
            for tb in range(4):
                wv = []
                if ic == 0 and tb == 0:
                    wv = [t_wkv, halfev[h], v_free[0]]
                c0 = h * 16384 + ic * 512 + tb * 128
                tv = mm(vps[:, tb * 128:(tb + 1) * 128],
                        big_b[:, c0:c0 + 128],
                        wkv_sb[:, ic, 128:256],
                        ic == 0 and tb == 0, ic == 31, wait=wv,
                        inc="pe" if (ic == 31 and tb == 3) else None)
        v_free[0] = E("dve", lambda e,
                      o=vtm[:, (tsp * 2 + h) * 512:(tsp * 2 + h + 1) * 512],
                      src=vps[:, 0:512]:
                      e.tensor_copy(o, src), wait=[tv], inc="dve")

    for tsp in range(4):
        t_wkv = dma("pool", wkv_sb, wkv_in.rearrange("(ic p) n -> p ic n",
                                                     p=128),
                    wait=[q_last[0]])
        for rs in range(8):
            gi = tsp * 8 + rs
            tok0 = tsp * 1024 + rs * 128
            lev = emit_strip(gi, x_in[tok0:tok0 + 128, :])
            if rs == 3 or rs == 7:
                halfev[rs // 4] = lev
                emit_kv(tsp, rs // 4, t_wkv)
        tq = None
        for slot in range(8):
            wp = slot % 2
            wv_w = [v_free[0]] if slot < 2 else [w_free[wp]]
            twl = dma("pool", wq_t[wp],
                      wq8_in[:, slot * 128:(slot + 1) * 128]
                      .rearrange("(ic p) c -> p ic c", p=128),
                      wait=wv_w)
            qa = qacc[2 * (slot % 2)]
            qb = qacc[2 * (slot % 2) + 1]
            for ic in range(32):
                wv = [halfev[0], halfev[1], twl, q_evac[2 * (slot % 2)],
                      q_evac[2 * (slot % 2) + 1]] if ic == 0 else []
                mm(qa[:, :], wq_t[wp][:, ic, :], bb(0, ic),
                   ic == 0, ic == 31, wait=wv)
                tq = mm(qb[:, :], wq_t[wp][:, ic, :], bb(1, ic),
                        ic == 0, ic == 31,
                        inc="pe" if ic == 31 else None)
            w_free[wp] = tq
            for hh in range(2):
                q_evac[2 * (slot % 2) + hh] = E(
                    "dve", lambda e, src=(qa if hh == 0 else qb),
                    o=big_a[:, slot * 4096 + tsp * 1024 + hh * 512:
                            slot * 4096 + tsp * 1024 + (hh + 1) * 512]:
                    e.tensor_copy(o, src[:, :]), wait=[tq], inc="dve")
        q_last[0] = tq
        bb_free[0] = tq
    own_done = None
    for rs in range(8):
        own_done = emit_strip(32 + rs, xown_in[rs * 128:(rs + 1) * 128, :])
    rec.flush("frontend")
    if stop_after == "frontend":
        return nc
    ps.close()

    ps = contextlib.ExitStack()
    eacc = [ps.enter_context(nc.psum_tensor("eacc%d" % i, [128, 512], F32))
            for i in range(2)]
    sps = [ps.enter_context(nc.psum_tensor("sps%d" % i, [128, 512], F32))
           for i in range(2)]
    esum = [ps.enter_context(nc.psum_tensor("esum%d" % i, [128, 512], F32))
            for i in range(2)]
    wexp = [ps.enter_context(nc.psum_tensor("wexp%d" % i, [128, 512], F32))
            for i in range(2)]
    w_free = [q_last[0], q_last[0]]
    ev = {}
    for cc in range(5):
        wp = cc % 2
        twl = dma("pool", wq_t[wp],
                  wqka_in[:, cc * 128:(cc + 1) * 128]
                  .rearrange("(ic p) c -> p ic c", p=128), wait=[w_free[wp]])
        for tb in range(2):
            tq = None
            for ic in range(32):
                wv = [twl, own_done,
                      ev.get((cc - 1, tb))] if ic == 0 else []
                tq = mm(eacc[tb][:, :], wq_t[wp][:, ic, :],
                        bb(tb, ic), ic == 0, ic == 31, wait=wv,
                        inc="pe" if ic == 31 else None)
            w_free[wp] = tq
            if cc < 2:
                dst = strip0[:, cc * 1024 + tb * 512:
                             cc * 1024 + (tb + 1) * 512]
                ev[(cc, tb)] = E("dve", lambda e, src=eacc[tb], o=dst:
                                 e.tensor_copy(o, src[:, :]), wait=[tq],
                                 inc="dve")
            elif cc < 4:
                dst = strip0[:, (cc - 2) * 1024 + tb * 512:
                             (cc - 2) * 1024 + (tb + 1) * 512]
                ev[(cc, tb)] = E("dve", lambda e, src=eacc[tb], o=dst:
                                 e.tensor_tensor(o, o, src[:, :], ALU.mult),
                                 wait=[tq, ev[(cc - 2, tb)]], inc="dve")
            else:
                dst = hcol[:, tb * 512:(tb + 1) * 512]
                ev[(cc, tb)] = E("dve", lambda e, src=eacc[tb], o=dst:
                                 e.tensor_copy(o, src[:, :]), wait=[tq],
                                 inc="dve")
    for tb in range(2):
        t_s = None
        for icc in range(2):
            t_s = mm(sps[tb][0:8, 0:512], bdiag_sb[:, icc * 8:(icc + 1) * 8],
                     strip0[:, icc * 1024 + tb * 512:
                            icc * 1024 + (tb + 1) * 512],
                     icc == 0, icc == 1,
                     wait=[ev[(icc + 2, tb)]],
                     inc="pe" if icc == 1 else None)
        t_exp = E("act", lambda e, i_=sps[tb][0:8, 0:512],
                  o=sexp_sb[0:8, tb * 512:(tb + 1) * 512]:
                  e.activation(out=o, in_=i_, func=AF.Exp),
                  wait=[t_s], inc="act")
        t_es = mm(esum[tb][0:1, 0:512], ones8_f[:, :],
                  sexp_sb[0:8, tb * 512:(tb + 1) * 512], True, True,
                  wait=[t_exp], inc="pe")
        t_we = mm(wexp[tb][:, :], prep_sb[:, :],
                  sexp_sb[0:8, tb * 512:(tb + 1) * 512], True, True,
                  wait=[t_exp], inc="pe")
        t_rc = E("dve", lambda e, i_=esum[tb][0:1, 0:512],
                 o=rsum_sb[0:1, tb * 512:(tb + 1) * 512]:
                 e.reciprocal(o, i_), wait=[t_es], inc="dve")
        t_rb = mm(sps[tb][:, :], onesr_f[:, :],
                  rsum_sb[0:1, tb * 512:(tb + 1) * 512], True, True,
                  wait=[t_rc], inc="pe")
        t_tmp = E("dve", lambda e, h=hcol[:, tb * 512:(tb + 1) * 512],
                  w_=wexp[tb], o=rb_sb[:, tb * 512:(tb + 1) * 512]:
                  e.tensor_tensor(o, h, w_[:, :], ALU.mult),
                  wait=[t_we, ev[(4, tb)]], inc="dve")
        E("dve", lambda e, t_=rb_sb[:, tb * 512:(tb + 1) * 512],
          r_=sps[tb], o=hw_sb[:, tb * 512:(tb + 1) * 512]:
          e.tensor_tensor(o, t_, r_[:, :], ALU.mult),
          wait=[t_rb, t_tmp], inc="dve")
    rec.flush("expert")
    if stop_after == "expert":
        return nc
    ps.close()

    ps = contextlib.ExitStack()
    scp = [ps.enter_context(nc.psum_tensor("scp%d" % i, [128, 512], F32))
           for i in range(4)]
    avp = [ps.enter_context(nc.psum_tensor("avp%d" % i, [128, 512], F32))
           for i in range(2)]
    dnp = [ps.enter_context(nc.psum_tensor("dnp%d" % i, [128, 512], F32))
           for i in range(2)]

    sc_free = [None] * 4
    probs_free = [()] * 4
    av_free = [None, None]
    dn_free = [None, None]
    rrow_free = [None, None]
    rb_free = [None, None]
    of_free = [None, None]

    def cc_fn(e, half):
        with allow_ici_all_to_all():
            return e.collective_compute(
                "AllToAll", ALU.bypass,
                replica_groups=[[0, 1, 2, 3, 4, 5, 6, 7]],
                ins=[att_send[half].opt()], outs=[att_recv[half].opt()])
    t_tri = dma("pool", strip_bf[:, 5120:6144], masks_in)
    wg = [stream_w[:, 0:4096].rearrange("p (k c) -> p k c", c=512),
          stream_w[:, 4096:8192].rearrange("p (k c) -> p k c", c=512)]
    pre_tg = {}
    for g0 in range(2):
        pre_tg[(0, g0)] = dma("pool", wg[g0 % 2],
                              wt_in[g0 * 1024:(g0 + 1) * 1024, 0:512]
                              .rearrange("(k p) c -> p k c", p=128),
                              wait=[w_free[g0 % 2]])
    pre_bm = dma("pool", bm_tile[:, 0:512], bm_in[:, 0:512])
    pre_br = dma("pool", brow_sb[0:1, 0:512], brow_in[0:1, 0:512])
    it = 0
    jcnt = [0]
    pending = []
    pending_av = []
    tail_j = [0]

    def emit_tail(pt):
        pq, t_cpd, col0 = pt
        t_rbm = mm(dnp[pq][:, :], onesr_f[:, :],
                   rrow[0:1, pq * 512:(pq + 1) * 512], True, True,
                   wait=[t_cpd], inc="pe")
        rrow_free[pq] = t_rbm
        t_rcp = E("dve", lambda e, i_=dnp[pq][:, :],
                  o=rb_sb[:, pq * 512:(pq + 1) * 512]:
                  e.reciprocal(o, i_),
                  wait=[t_rbm, rb_free[pq]], inc="dve")
        dn_free[pq] = t_rcp
        t_ev = E("dve", lambda e, a=avp[pq][:, :],
                 r_=rb_sb[:, pq * 512:(pq + 1) * 512],
                 o=big_a[:, col0:col0 + 512]:
                 e.tensor_tensor(o, a, r_, ALU.mult),
                 wait=[t_rcp], inc="dve")
        av_free[pq] = t_ev
        rb_free[pq] = t_ev
        return t_ev

    def emit_av(pv):
        j_, kb_, texp_, pq_, nkb_ = pv
        pr = probs[:, j_ * 512:(j_ + 1) * 512]
        mm(avp[pq_][:, :], vtm[:, kb_ * 128:(kb_ + 1) * 128], pr,
           kb_ == 0, kb_ == nkb_ - 1,
           wait=[texp_, av_free[pq_] if kb_ == 0 else None])
        td = mm(dnp[pq_][:, :], onesc_bf[:, :], pr,
                kb_ == 0, kb_ == nkb_ - 1,
                wait=[dn_free[pq_] if kb_ == 0 else None], inc="pe")
        probs_free[j_] = (td,)
        return td

    for slot in range(8):
        qcol0 = slot * 4096
        slot_evac = None
        for g in range(8):
            gb = 512 * g
            nkb = 4 * (g + 1)
            pq = it % 2
            if slot < 3:
                t_of = dma("sp", offr3_sb[0:3, pq * 512:(pq + 1) * 512],
                           offr3_in[(slot * 8 + g) * 3:
                                    (slot * 8 + g + 1) * 3, :],
                           wait=[of_free[pq]])
            t_dn = None
            for kb in range(nkb):
                j = jcnt[0] % 4
                jcnt[0] += 1
                diag = kb >= gb // 128
                stab = slot < 3
                wv = [sc_free[j]]
                if kb == 0:
                    wv.append(t_tri)
                    if stab:
                        wv.append(t_of)
                last = not diag and not stab
                tsc = mm(scp[j][:, :],
                         kT[:, kb * 128:(kb + 1) * 128],
                         big_a[:, qcol0 + gb:qcol0 + gb + 512],
                         True, last, wait=wv,
                         inc="pe" if last else None)
                if diag:
                    moff = 384 - 128 * (kb - gb // 128)
                    tsc = mm(scp[j][:, :],
                             strip_bf[:, 6016:6144],
                             strip_bf[:, 5120 + moff:5120 + moff + 512],
                             False, not stab,
                             inc=None if stab else "pe")
                if stab:
                    tsc = mm(scp[j][:, :], ones3[0:3, :],
                             offr3_sb[0:3, pq * 512:(pq + 1) * 512],
                             False, True, inc="pe")
                if len(pending_av) >= 2:
                    emit_av(pending_av.pop(0))
                if slot < 3:
                    c = BCOLS[(slot, "k", kb)]
                else:
                    c = BCOLS[(slot, (g + 1) * 512 - 128 * kb)]
                texp = E("act", lambda e, i_=scp[j][:, :],
                         o=probs[:, j * 512:(j + 1) * 512],
                         b=btab_sb[:, c:c + 1]:
                         e.activation(out=o, in_=i_, func=AF.Exp,
                                      scale=QK_SCALE, bias=b),
                         wait=[tsc, t_btab] + list(probs_free[j]),
                         inc="act")
                sc_free[j] = texp
                if kb == 1 and pending:
                    emit_tail(pending.pop(0))
                pending_av.append((j, kb, texp, pq, nkb))
            while pending_av:
                t_dn = emit_av(pending_av.pop(0))
            if slot < 3:
                of_free[pq] = tsc
            t_cpd = E("act", lambda e, i_=dnp[pq][0:1, :],
                      o=rrow[0:1, pq * 512:(pq + 1) * 512]:
                      e.copy(o, i_),
                      wait=[t_dn, rrow_free[pq]], inc="act")
            dn_free[pq] = t_cpd
            pending.append((pq, t_cpd, qcol0 + gb))
            it += 1
        while pending:
            slot_evac = emit_tail(pending.pop(0))
        half = slot // 4
        pr_ = slot % 4
        for mb in range(8):
            m = mb % 4
            dma("sp", att_send[half][mb, pr_ * 128:(pr_ + 1) * 128, :],
                big_a[:, slot * 4096 + m * 1024:
                      slot * 4096 + (m + 1) * 1024],
                wait=[slot_evac])
        if slot == 3:
            t_cca = E("pool", lambda e: cc_fn(e, 0),
                      wait=[("d_sp%d" % l, rec.cnt["d_sp%d" % l])
                            for l in range(N_LANES)], inc="cc")

    t_ccb = E("pool", lambda e: cc_fn(e, 1),
              wait=[("d_sp%d" % l, rec.cnt["d_sp%d" % l])
                    for l in range(N_LANES)], inc="cc")
    rec.flush("attn")
    if stop_after == "attn":
        return nc
    ps.close()

    ps = contextlib.ExitStack()
    po = [ps.enter_context(nc.psum_tensor("po%d" % i, [128, 512], F32))
          for i in range(8)]

    attn_lds = []

    def _grp_reg(e):
        if "grp" not in rec.rtv:
            rec.rtv["grp"] = (e.partition_id() // 4) % 2
        return rec.rtv["grp"]

    for s_l in range(4):
        for half in range(2):
            def ld_fn(e, s_l=s_l, half=half):
                g2 = _grp_reg(e)
                r2 = att_recv[half].rearrange("m r t -> (m r) t")
                blk_ = r2[bass.ts(g2, 2048), :][s_l * 512:(s_l + 1) * 512, :]
                return e.dma_start(
                    out=big_a[:, s_l * 8192 + half * 4096:
                              s_l * 8192 + (half + 1) * 4096]
                    .rearrange("p (c t) -> p c t", t=1024),
                    in_=blk_.rearrange("(c p) t -> p c t", p=128))
            lane, pre = _lane("sp")
            t_attn = E("sp", ld_fn,
                       wait=[("cc", 1 if half == 0 else 2), pre],
                       inc=lane, step=16)
            attn_lds.append(t_attn)

    def lhs_for(g, k, tc):
        if g >= 4:
            i = (g - 4) * 8 + k
            half = i // 16
            r = i % 16
            c0 = (r // 4) * 8192 + half * 4096 + (r % 4) * 1024 + tc * 128
            return big_a[:, c0:c0 + 128]
        ic = g * 8 + k
        c0 = (tc // 4) * 16384 + ic * 512 + (tc % 4) * 128
        return big_b[:, c0:c0 + 128]

    grp_free = [None, None]
    po_free = [slot_evac] * 8
    x_free = [None] * 8
    out_free = [None, None]
    bm_free = None
    for ob in range(8):
        obp = ob % 2
        xl = []
        for tc in range(8):
            xl.append(dma(
                "sp", strip0[:, tc * 512:(tc + 1) * 512],
                xown_in[tc * 128:(tc + 1) * 128, ob * 512:(ob + 1) * 512],
                wait=[x_free[tc]]))
        if ob == 0:
            t_bm, t_br = pre_bm, pre_br
        else:
            t_bm = dma("pool", bm_tile[:, 0:512],
                       bm_in[:, ob * 512:(ob + 1) * 512], wait=[bm_free])
            t_br = dma("pool", brow_sb[0:1, 0:512],
                       brow_in[0:1, ob * 512:(ob + 1) * 512], wait=[bm_free])
        for g in range(8):
            sw = g % 2
            if g >= 4:
                src = wo_in[(g - 4) * 1024:(g - 3) * 1024,
                            ob * 512:(ob + 1) * 512]
            else:
                src = wt_in[g * 1024:(g + 1) * 1024,
                            ob * 512:(ob + 1) * 512]
            if (ob, g) in pre_tg:
                tg = pre_tg[(ob, g)]
            else:
                tg = dma("pool", wg[sw],
                         src.rearrange("(k p) c -> p k c", p=128),
                         wait=[grp_free[sw]])
            if g < 7:
                last = None
                for tc in range(8):
                    for k in range(8):
                        wv = []
                        if tc == 0 and k == 0:
                            wv.append(tg)
                            if ob == 0 and g == 4:
                                wv.extend(attn_lds[0::2])
                            if ob == 0 and g == 6:
                                wv.extend(attn_lds[1::2])
                        if g == 0 and k == 0:
                            wv.append(po_free[tc])
                        last = mm(po[tc][:, :], lhs_for(g, k, tc),
                                  wg[sw][:, k, :],
                                  g == 0 and k == 0, False, wait=wv,
                                  inc="pe" if (tc == 7 and k == 7) else None)
                grp_free[sw] = last
            else:
                t_last_tail = None
                for tc in range(8):
                    for k in range(8):
                        wv = []
                        if tc == 0 and k == 0:
                            wv.append(tg)
                        mm(po[tc][:, :], lhs_for(7, k, tc), wg[sw][:, k, :],
                           False, False, wait=wv)
                    mm(po[tc][:, :], hw_sb[:, tc * 128:(tc + 1) * 128],
                       bm_tile[:, 0:512], False, False,
                       wait=[t_bm])
                    t_tail = mm(po[tc][:, :], onesr_bf[:, :],
                                brow_sb[0:1, 0:512],
                                False, True,
                                wait=[t_br], inc="pe")
                    t_last_tail = t_tail
                    oc = (ob * 8 + tc) % 2
                    t_ev = E("dve", lambda e, a=po[tc],
                             x_=strip0[:, tc * 512:(tc + 1) * 512],
                             o=out_sb[:, oc * 512:(oc + 1) * 512]:
                             e.tensor_tensor(o, a[:, :], x_, ALU.add),
                             wait=[t_tail, xl[tc], out_free[oc]],
                             inc="dve")
                    po_free[tc] = t_ev
                    x_free[tc] = t_ev
                    t_st = dma("act",
                               out_dram[tc * 128:(tc + 1) * 128,
                                        ob * 512:(ob + 1) * 512],
                               out_sb[:, oc * 512:(oc + 1) * 512],
                               wait=[t_ev])
                    out_free[oc] = t_st
                grp_free[sw] = t_last_tail
        bm_free = t_last_tail
    E("sp", None, wait=[("%s%d" % (q, l), rec.cnt["%s%d" % (q, l)])
                        for q in ("d_act", "d_pool", "d_sp")
                        for l in range(N_LANES)])
    rec.flush("epilogue")
    ps.close()
    return nc


_PROG = None


def _consts():
    masks = np.zeros((128, 1024), dtype=np.float32)
    c = np.arange(128)
    t = np.arange(896)
    band = (c[:, None] == t[None, :] - 383).astype(np.float32)
    band[0, :384] = 1.0
    masks[:, :896] = band * -1e9
    k = np.arange(128)
    masks[:, 896:1024] = (k[None, :] >= c[:, None]).astype(np.float32)
    prep = np.zeros((8, 128), dtype=np.float32)
    for e in range(8):
        prep[e, e * 16:(e + 1) * 16] = 1.0
    bdiag = np.zeros((128, 16), dtype=np.float32)
    p_arr = np.arange(128)
    for e in range(4):
        bdiag[p_arr // 32 == e, e] = ISC32
    for e in range(4, 8):
        bdiag[p_arr // 32 == e - 4, 8 + e] = ISC32
    ident = np.eye(128, dtype=np.float32)
    return masks, prep, bdiag, ident


def _split3(v):
    h1 = v.astype(BF)
    r = v - h1.astype(np.float64)
    h2 = r.astype(BF)
    h3 = (r - h2.astype(np.float64)).astype(BF)
    return h1, h2, h3


def _bias_table(g):
    bt = np.zeros((128, NBC), dtype=np.float32)
    p = np.arange(128, dtype=np.float64)
    for key, c in BCOLS.items():
        s = key[0]
        sl = SLOPES[4 * s + g]
        if len(key) == 2:
            delta = min(sl * (NQ_SLOT[s] - 1) / 2.0, DELTA_CAP)
            bt[:, c] = (sl * (p - key[1]) + delta).astype(np.float32)
        else:
            bt[:, c] = (sl * (128 * key[2] + p)).astype(np.float32)
    return bt


def _offr3_table(g):
    t = np.zeros((24, 3, 512), dtype=BF)
    for s in range(3):
        sl = SLOPES[4 * s + g]
        nq = NQ_SLOT[s]
        delta = min(sl * (nq - 1) / 2.0, DELTA_CAP)
        for grp in range(8):
            qabs = grp * 512 + np.arange(512, dtype=np.float64)
            qhi = (np.floor(qabs / nq) + 1.0) * nq
            off = (delta - sl * qhi) / QK_SCALE
            o1, o2, o3 = _split3(off)
            t[s * 8 + grp, 0], t[s * 8 + grp, 1], t[s * 8 + grp, 2] = \
                o1, o2, o3
    return np.ascontiguousarray(t.reshape(72, 512))


def kernel(L0_attn_hidden_states, L0_attn_Wqkv, L0_attn_Wo, L0_ffn_W,
           L0_ffn_b, L0_ffn_Wq, L0_ffn_Wk, L0_ffn_A, L0_ffn_Bm):
    global _PROG
    x = np.ascontiguousarray(np.asarray(L0_attn_hidden_states,
                                        dtype=np.float32))
    wqkv = np.ascontiguousarray(np.asarray(L0_attn_Wqkv, dtype=np.float32))
    wo = np.asarray(L0_attn_Wo, dtype=np.float32)
    w = np.asarray(L0_ffn_W, dtype=np.float32)
    bv = np.asarray(L0_ffn_b, dtype=np.float32)
    wq = np.asarray(L0_ffn_Wq, dtype=np.float32)
    wk = np.asarray(L0_ffn_Wk, dtype=np.float32)
    a_ = np.asarray(L0_ffn_A, dtype=np.float32)
    bm = np.asarray(L0_ffn_Bm, dtype=np.float32)

    perm = np.concatenate([np.arange(128) + (4 * (hf * 4 + cc_) + sl) * 128
                           for hf in range(2) for sl in range(4)
                           for cc_ in range(4)])
    wo_p = np.ascontiguousarray(wo[perm, :])
    wt = np.ascontiguousarray(w.T)
    wqka = np.ascontiguousarray(
        np.concatenate([wq, wk, a_.reshape(128, DM)], axis=0).T)
    bm_f = np.ascontiguousarray(
        bm.transpose(0, 2, 1).reshape(128, DM))
    brow = np.ascontiguousarray(bv.reshape(1, DM))
    masks, prep, bdiag, ident = _consts()

    in_maps = []
    for c in range(8):
        b, g = c // 4, c % 4
        btab = _bias_table(g)
        offr3 = _offr3_table(g)
        wq8 = np.ascontiguousarray(np.concatenate(
            [wqkv[:, (4 * p + g) * 128:(4 * p + g + 1) * 128]
             for p in range(8)], axis=1))
        in_maps.append({
            "x": x[b],
            "x_own": np.ascontiguousarray(x[b][1024 * g:1024 * (g + 1)]),
            "wq8": wq8,
            "wkv": np.ascontiguousarray(wqkv[:, 4096:4352]),
            "wo": wo_p,
            "wt": wt,
            "brow": brow,
            "wqka": wqka,
            "bm": bm_f,
            "prep": prep,
            "bdiag": bdiag,
            "masks": masks,
            "btab": btab,
            "offr3": offr3,
            "ident": ident,
        })
    if _PROG is None:
        _PROG = build_program()
    res = run_bass_kernel_spmd(_PROG, in_maps, core_ids=list(range(8)))
    out = np.empty((2, S, DM), dtype=np.float32)
    for c in range(8):
        b, g = c // 4, c % 4
        out[b, 1024 * g:1024 * (g + 1), :] = np.asarray(
            res.results[c]["out"])
    return out


# version 27
# speedup vs baseline: 1.0038x; 1.0038x over previous
import math
import numpy as np
import ml_dtypes
import concourse.bass as bass
import concourse.mybir as mybir
from concourse.bass_utils import run_bass_kernel_spmd
from concourse.aot_env import allow_ici_all_to_all

F32 = mybir.dt.float32
BF16 = mybir.dt.bfloat16
AF = mybir.ActivationFunctionType
ALU = mybir.AluOpType
AX = mybir.AxisListType
BF = ml_dtypes.bfloat16

S, DM, HD, NH = 4096, 4096, 128, 32
EPS = 1e-5
QK_SCALE = 1.0 / math.sqrt(HD)
NQ_SLOT = [64, 128, 256, 512, 512, 512, 512, 512]
DELTA_CAP = 25.0
ISC32 = 1.0 / math.sqrt(32.0)
SLOPES = [2.0 ** (-0.25 * (i + 1)) for i in range(NH)]

MASK_SPECS = [(512, 0), (512, 1), (512, 2), (512, 3), (256, 0), (256, 1),
              (128, 0), (64, 0), (64, 64)]
MASK_OFF = {}
_o = 0
for _nq, _a in MASK_SPECS:
    MASK_OFF[(_nq, _a)] = _o
    _o += _nq
MASK_W = _o

BCOLS = {}
for _s in range(3, 8):
    for _g in range(8):
        for _kb in range(4 * (_g + 1)):
            _d = (_g + 1) * 512 - 128 * _kb
            if (_s, _d) not in BCOLS:
                BCOLS[(_s, _d)] = len(BCOLS)
for _s in range(3):
    for _kb in range(32):
        BCOLS[(_s, "k", _kb)] = len(BCOLS)
NBC = len(BCOLS)

ENGS = ("sp", "pool", "act", "dve", "pe")
_BLK = {"sp": "sync", "pool": "gpsimd", "act": "scalar", "dve": "vector",
        "pe": "tensor"}


N_LANES = 4
DBG_STATIC_RECV = False


class Rec:
    def __init__(self, nc):
        self.nc = nc
        names = ["pe", "act", "dve", "pool", "cc"]
        for q in ("d_sp", "d_act", "d_pool"):
            for l in range(N_LANES):
                names.append("%s%d" % (q, l))
        self.sems = {n: nc.alloc_semaphore("s_" + n) for n in names}
        self.cnt = {n: 0 for n in names}
        self.waited = {e: {n: 0 for n in names} for e in ENGS}
        self.ops = {e: [] for e in ENGS}
        self.qidx = {"d_sp": 0, "d_act": 0, "d_pool": 0}
        self.rtv = {}

    def emit(self, eng, fn, wait=(), inc=None, step=1):
        ws = []
        for t in wait:
            if t is None:
                continue
            s, v = t
            if v > self.waited[eng][s]:
                ws.append((self.sems[s], v))
                self.waited[eng][s] = v
        tok = None
        semh = None
        if inc is not None:
            self.cnt[inc] += step
            tok = (inc, self.cnt[inc])
            semh = self.sems[inc]

        def closure(e, ws=tuple(ws), fn=fn, semh=semh, step=step):
            for sh, v in ws:
                e.wait_ge(sh, v)
            if fn is not None:
                ins = fn(e)
                if semh is not None:
                    ins.then_inc(semh, step)
        self.ops[eng].append(closure)
        return tok

    def flush(self, name):
        with self.nc.Block(name) as blk:
            for e in ENGS:
                if self.ops[e]:
                    def section(eng, lst=tuple(self.ops[e])):
                        for f in lst:
                            f(eng)
                    getattr(blk, _BLK[e])(section)
        self.ops = {e: [] for e in ENGS}
        for e in ENGS:
            for s in ("pe", "act", "dve", "pool"):
                self.waited[e][s] = max(self.waited[e][s], self.cnt[s])


def build_program(stop_after=None):
    import contextlib
    nc = bass.Bass("TRN2", target_bir_lowering=False)

    x_in = nc.dram_tensor("x", [S, DM], F32, kind="ExternalInput").ap()
    xown_in = nc.dram_tensor("x_own", [1024, DM], F32, kind="ExternalInput").ap()
    wq8_in = nc.dram_tensor("wq8", [DM, 1024], F32, kind="ExternalInput").ap()
    wkv_in = nc.dram_tensor("wkv", [DM, 256], F32, kind="ExternalInput").ap()
    wo_in = nc.dram_tensor("wo", [DM, DM], F32, kind="ExternalInput").ap()
    wt_in = nc.dram_tensor("wt", [DM, DM], F32, kind="ExternalInput").ap()
    brow_in = nc.dram_tensor("brow", [1, DM], F32, kind="ExternalInput").ap()
    wqka_in = nc.dram_tensor("wqka", [DM, 640], F32, kind="ExternalInput").ap()
    bm_in = nc.dram_tensor("bm", [128, DM], F32, kind="ExternalInput").ap()
    prep_in = nc.dram_tensor("prep", [8, 128], F32, kind="ExternalInput").ap()
    bdiag_in = nc.dram_tensor("bdiag", [128, 16], F32, kind="ExternalInput").ap()
    masks_in = nc.dram_tensor("masks", [128, 1024], F32, kind="ExternalInput").ap()
    btab_in = nc.dram_tensor("btab", [128, NBC], F32,
                             kind="ExternalInput").ap()
    offr3_in = nc.dram_tensor("offr3", [72, 512], BF16,
                              kind="ExternalInput").ap()
    ident_in = nc.dram_tensor("ident", [128, 128], F32, kind="ExternalInput").ap()
    out_dram = nc.dram_tensor("out", [1024, DM], F32, kind="ExternalOutput").ap()

    att_send = [nc.dram_tensor("att_send%d" % i, [8, 512, 1024], BF16,
                                kind="Internal").ap() for i in range(2)]
    att_recv = [nc.dram_tensor("att_recv%d" % i, [8, 512, 1024], BF16,
                               kind="Internal").ap() for i in range(2)]

    A = nc.alloc_sbuf_tensor
    big_a = A("big_a", [128, 32768], BF16)
    big_b = A("big_b", [128, 32768], BF16)
    stream_w = A("stream_w", [128, 8192], BF16)
    strip0 = A("strip0", [128, 4096], F32)
    scratch = A("scratch", [128, 4352], F32)
    scratch_bf = scratch.bitcast(BF16)

    class _OffView:
        def __init__(self, h, off, width):
            self.h, self.off, self.width = h, off, width

        def __getitem__(self, idx):
            rows, cols = idx
            a = cols.start if cols.start is not None else 0
            b = cols.stop if cols.stop is not None else self.width
            return self.h[rows, self.off + a:self.off + b]

    strip = [strip0, _OffView(scratch, 0, 4096)]
    probs = _OffView(scratch_bf, 0, 2048)
    rb_sb = _OffView(scratch, 1024, 1024)
    rrow = _OffView(scratch, 2048, 1024)
    hcol = _OffView(scratch_bf, 6144, 1024)
    hw_sb = _OffView(scratch_bf, 7168, 1024)
    bm_tile = _OffView(scratch_bf, 8192, 512)
    xnstage = A("xnstage", [128, 4096], BF16)
    kT = A("kT", [128, S], BF16)
    vtm = A("vtm", [128, S], BF16)
    strip_bf = strip0.bitcast(BF16)
    identb = A("identb", [128, 128], BF16)
    prep_sb = A("prep_sb", [8, 128], F32)
    bdiag_sb = A("bdiag_sb", [128, 16], F32)
    brow_sb = A("brow_sb", [1, 512], BF16)
    onesc_bf = A("onesc_bf", [128, 128], BF16)
    ones3 = A("ones3", [3, 128], BF16)
    offr3_sb = A("offr3_sb", [3, 1024], BF16)
    btab_sb = A("btab_sb", [128, NBC], F32)
    onesr_bf = A("onesr_bf", [1, 128], BF16)
    onesr_f = A("onesr_f", [1, 128], F32)
    ones8_f = A("ones8_f", [8, 1], F32)
    sexp_sb = rb_sb
    rsum_sb = rrow
    out_sb = rb_sb
    st_bn = A("st_bn", [128, 96], F32)
    st_mv = A("st_mv", [128, 4], F32)
    st_ln = A("st_ln", [128, 2], F32)
    st_r = A("st_r", [128, 2], F32)
    st_mr = A("st_mr", [128, 2], F32)
    st_eps = A("st_eps", [128, 1], F32)

    rec = Rec(nc)
    E = rec.emit
    DQ = {"sp": "d_sp", "act": "d_act", "pool": "d_pool"}

    def _lane(q):
        base = DQ[q]
        lane = "%s%d" % (base, rec.qidx[base] % N_LANES)
        rec.qidx[base] += 1
        pre = (lane, rec.cnt[lane])
        return lane, pre

    def dma(q, out, in_, wait=()):
        lane, pre = _lane(q)
        return E(q, lambda e, o=out, i=in_: e.dma_start(out=o, in_=i),
                 wait=list(wait) + [pre], inc=lane, step=16)

    def dma_t(q, out, in_, wait=()):
        lane, pre = _lane(q)
        return E(q, lambda e, o=out, i=in_: e.dma_start(out=o, in_=i,
                 transpose=True), wait=list(wait) + [pre], inc=lane, step=16)

    def mm(out, lhsT, rhs, start, stop, wait=(), inc=None):
        return E("pe", lambda e, o=out, l=lhsT, r=rhs, st=start, sp=stop:
                 e.matmul(o, l, r, start=st, stop=sp, skip_group_check=True),
                 wait=wait, inc=inc)

    t_ident = dma("pool", identb[:, :], ident_in)
    dma("sp", prep_sb[:, :], prep_in)
    t_btab = dma("sp", btab_sb[:, :], btab_in)
    dma("sp", bdiag_sb[:, :], bdiag_in)
    E("pool", lambda e: e.memset(st_eps[:, :], EPS), inc="pool")
    E("pool", lambda e: e.memset(onesc_bf[:, :], 1.0), inc="pool")
    E("pool", lambda e: e.memset(ones3[:, :], 1.0), inc="pool")
    E("pool", lambda e: e.memset(onesr_bf[:, :], 1.0), inc="pool")
    E("pool", lambda e: e.memset(onesr_f[:, :], 1.0), inc="pool")
    E("pool", lambda e: e.memset(ones8_f[:, :], 1.0), inc="pool")
    rec.flush("init")
    if stop_after == "init":
        return nc

    ps = contextlib.ExitStack()
    tps = [ps.enter_context(nc.psum_tensor("tps%d" % i, [128, 512], BF16))
           for i in range(2)]
    kacc = ps.enter_context(nc.psum_tensor("kacc", [128, 512], F32))
    vps = ps.enter_context(nc.psum_tensor("vps", [128, 512], F32))
    qacc = [ps.enter_context(nc.psum_tensor("qacc%d" % i, [128, 512], F32))
            for i in range(4)]

    def bb(h, ic):
        return big_b[:, h * 16384 + ic * 512:h * 16384 + (ic + 1) * 512]

    def bb4(h, ic0, toff):
        return (big_b[:, h * 16384:(h + 1) * 16384]
                .rearrange("p (ic t) -> p ic t", t=512)
                [:, ic0:ic0 + 4, toff:toff + 128])

    strip_free = [(), ()]
    st_free = [(), ()]
    mv_free = [(), ()]
    r_free = [None, None]
    xs_free = [None] * 8
    tps_free = [None, None]
    xscnt = [0]
    bb_free = [None]
    halfev = [None, None]
    k_free = [None]
    v_free = [None]
    w_free = [None, None]
    q_evac = [None, None, None, None]
    q_last = [None]
    wkv_sb = stream_w[:, :].rearrange("p (ic n) -> p ic n", n=256)
    wq_t = [stream_w[:, 0:4096].rearrange("p (ic c) -> p ic c", c=128),
            stream_w[:, 4096:8192].rearrange("p (ic c) -> p ic c", c=128)]

    def emit_strip(gi, src_rows):
        par = gi % 2
        rs = gi % 8
        h = rs // 4
        toff = (rs % 4) * 128
        tlA = dma("sp", strip[par][:, 0:2048], src_rows[:, 0:2048],
                  wait=strip_free[par])
        tlB = dma("act", strip[par][:, 2048:4096], src_rows[:, 2048:4096],
                  wait=strip_free[par])
        tb0 = None
        for c in range(8):
            w = [tlA if c < 4 else tlB]
            if c == 0:
                w += list(st_free[par])
            tb0 = E("dve", lambda e, o=st_bn[:, par * 48 + c * 6:
                                             par * 48 + (c + 1) * 6],
                    s=strip[par][:, c * 512:(c + 1) * 512]:
                    e.bn_stats(o, s), wait=w, inc="dve")
        t_ag = E("dve", lambda e, o=st_mv[:, 2 * par:2 * par + 2],
                 i_=st_bn[:, par * 48:par * 48 + 48]:
                 e.bn_aggr(o, i_), wait=[tb0] + list(mv_free[par]),
                 inc="dve")
        st_free[par] = (t_ag,)
        t_ln = E("act", lambda e, i_=st_mv[:, 2 * par + 1:2 * par + 2],
                 o=st_ln[:, par:par + 1]:
                 e.activation(out=o, in_=i_, func=AF.Ln,
                              bias=st_eps[:, 0:1]),
                 wait=[t_ag], inc="act")
        t_r = E("act", lambda e, i_=st_ln[:, par:par + 1],
                o=st_r[:, par:par + 1]:
                e.activation(out=o, in_=i_, func=AF.Exp, scale=-0.5),
                wait=[t_ln, r_free[par]], inc="act")
        t_mr = E("dve", lambda e, m=st_mv[:, 2 * par:2 * par + 1],
                 r_=st_r[:, par:par + 1], o=st_mr[:, par:par + 1]:
                 e.scalar_tensor_tensor(o, m, -1.0, r_, ALU.mult, ALU.mult),
                 wait=[t_r], inc="dve")
        mv_free[par] = (t_mr, t_ln)
        last_nm = None
        last_ev = None
        for blk in range(8):
            xs = xscnt[0] % 8
            xscnt[0] += 1
            t_nm = E("pool", lambda e,
                     o=xnstage[:, xs * 512:(xs + 1) * 512],
                     s=strip[par][:, blk * 512:(blk + 1) * 512],
                     r_=st_r[:, par:par + 1], b=st_mr[:, par:par + 1]:
                     e.tensor_scalar(o, s, r_, b, ALU.mult, ALU.add),
                     wait=[t_mr, tlA if blk < 4 else tlB, xs_free[xs]],
                     inc="pool")
            last_nm = t_nm
            tp = blk % 2
            ttr = None
            for i in range(4):
                wv = [t_nm, tps_free[tp], t_ident] if i == 0 else []
                ttr = E("pe", lambda e, o=tps[tp][:, i * 128:(i + 1) * 128],
                        i_=xnstage[:, xs * 512 + i * 128:
                                   xs * 512 + (i + 1) * 128]:
                        e.transpose(o, i_, identb[:, :]),
                        wait=wv, inc="pe" if i == 3 else None)
            xs_free[xs] = ttr
            t_ev = E("dve", lambda e,
                     o=bb4(h, blk * 4, toff),
                     i_=tps[tp][:, 0:512].rearrange("p (i t) -> p i t",
                                                    t=128):
                     e.tensor_copy(o, i_),
                     wait=[ttr, bb_free[0]], inc="dve")
            tps_free[tp] = t_ev
            last_ev = t_ev
        strip_free[par] = (last_nm, tb0)
        r_free[par] = last_nm
        return last_ev

    def emit_kv(tsp, h, t_wkv):
        tk = None
        for ic in range(32):
            wv = [t_wkv, halfev[h], k_free[0]] if ic == 0 else []
            tk = mm(kacc[:, :], wkv_sb[:, ic, 0:128], bb(h, ic),
                    ic == 0, ic == 31, wait=wv,
                    inc="pe" if ic == 31 else None)
        k_free[0] = E("dve", lambda e,
                      o=kT[:, (tsp * 2 + h) * 512:(tsp * 2 + h + 1) * 512],
                      src=kacc[:, :]:
                      e.tensor_copy(o, src), wait=[tk], inc="dve")
        tv = None
        for ic in range(32):
            for tb in range(4):
                wv = []
                if ic == 0 and tb == 0:
                    wv = [t_wkv, halfev[h], v_free[0]]
                c0 = h * 16384 + ic * 512 + tb * 128
                tv = mm(vps[:, tb * 128:(tb + 1) * 128],
                        big_b[:, c0:c0 + 128],
                        wkv_sb[:, ic, 128:256],
                        ic == 0 and tb == 0, ic == 31, wait=wv,
                        inc="pe" if (ic == 31 and tb == 3) else None)
        v_free[0] = E("dve", lambda e,
                      o=vtm[:, (tsp * 2 + h) * 512:(tsp * 2 + h + 1) * 512],
                      src=vps[:, 0:512]:
                      e.tensor_copy(o, src), wait=[tv], inc="dve")

    for tsp in range(4):
        t_wkv = dma("pool", wkv_sb, wkv_in.rearrange("(ic p) n -> p ic n",
                                                     p=128),
                    wait=[q_last[0]])
        for rs in range(8):
            gi = tsp * 8 + rs
            tok0 = tsp * 1024 + rs * 128
            lev = emit_strip(gi, x_in[tok0:tok0 + 128, :])
            if rs == 3 or rs == 7:
                halfev[rs // 4] = lev
                emit_kv(tsp, rs // 4, t_wkv)
        tq = None
        for slot in range(8):
            wp = slot % 2
            wv_w = [v_free[0]] if slot < 2 else [w_free[wp]]
            twl = dma("pool", wq_t[wp],
                      wq8_in[:, slot * 128:(slot + 1) * 128]
                      .rearrange("(ic p) c -> p ic c", p=128),
                      wait=wv_w)
            qa = qacc[2 * (slot % 2)]
            qb = qacc[2 * (slot % 2) + 1]
            for ic in range(32):
                wv = [halfev[0], halfev[1], twl, q_evac[2 * (slot % 2)],
                      q_evac[2 * (slot % 2) + 1]] if ic == 0 else []
                mm(qa[:, :], wq_t[wp][:, ic, :], bb(0, ic),
                   ic == 0, ic == 31, wait=wv)
                tq = mm(qb[:, :], wq_t[wp][:, ic, :], bb(1, ic),
                        ic == 0, ic == 31,
                        inc="pe" if ic == 31 else None)
            w_free[wp] = tq
            for hh in range(2):
                q_evac[2 * (slot % 2) + hh] = E(
                    "dve", lambda e, src=(qa if hh == 0 else qb),
                    o=big_a[:, slot * 4096 + tsp * 1024 + hh * 512:
                            slot * 4096 + tsp * 1024 + (hh + 1) * 512]:
                    e.tensor_copy(o, src[:, :]), wait=[tq], inc="dve")
        q_last[0] = tq
        bb_free[0] = tq
    own_done = None
    for rs in range(8):
        own_done = emit_strip(32 + rs, xown_in[rs * 128:(rs + 1) * 128, :])
    ps.close()

    ps = contextlib.ExitStack()
    eacc = [ps.enter_context(nc.psum_tensor("eacc%d" % i, [128, 512], F32))
            for i in range(2)]
    sps = [ps.enter_context(nc.psum_tensor("sps%d" % i, [128, 512], F32))
           for i in range(2)]
    esum = [ps.enter_context(nc.psum_tensor("esum%d" % i, [128, 512], F32))
            for i in range(2)]
    wexp = [ps.enter_context(nc.psum_tensor("wexp%d" % i, [128, 512], F32))
            for i in range(2)]
    w_free = [q_last[0], q_last[0]]
    ev = {}
    for cc in range(5):
        wp = cc % 2
        twl = dma("pool", wq_t[wp],
                  wqka_in[:, cc * 128:(cc + 1) * 128]
                  .rearrange("(ic p) c -> p ic c", p=128), wait=[w_free[wp]])
        for tb in range(2):
            tq = None
            for ic in range(32):
                wv = [twl, own_done,
                      ev.get((cc - 1, tb))] if ic == 0 else []
                tq = mm(eacc[tb][:, :], wq_t[wp][:, ic, :],
                        bb(tb, ic), ic == 0, ic == 31, wait=wv,
                        inc="pe" if ic == 31 else None)
            w_free[wp] = tq
            if cc < 2:
                dst = strip0[:, cc * 1024 + tb * 512:
                             cc * 1024 + (tb + 1) * 512]
                ev[(cc, tb)] = E("dve", lambda e, src=eacc[tb], o=dst:
                                 e.tensor_copy(o, src[:, :]), wait=[tq],
                                 inc="dve")
            elif cc < 4:
                dst = strip0[:, (cc - 2) * 1024 + tb * 512:
                             (cc - 2) * 1024 + (tb + 1) * 512]
                ev[(cc, tb)] = E("dve", lambda e, src=eacc[tb], o=dst:
                                 e.tensor_tensor(o, o, src[:, :], ALU.mult),
                                 wait=[tq, ev[(cc - 2, tb)]], inc="dve")
            else:
                dst = hcol[:, tb * 512:(tb + 1) * 512]
                ev[(cc, tb)] = E("dve", lambda e, src=eacc[tb], o=dst:
                                 e.tensor_copy(o, src[:, :]), wait=[tq],
                                 inc="dve")
    for tb in range(2):
        t_s = None
        for icc in range(2):
            t_s = mm(sps[tb][0:8, 0:512], bdiag_sb[:, icc * 8:(icc + 1) * 8],
                     strip0[:, icc * 1024 + tb * 512:
                            icc * 1024 + (tb + 1) * 512],
                     icc == 0, icc == 1,
                     wait=[ev[(icc + 2, tb)]],
                     inc="pe" if icc == 1 else None)
        t_exp = E("act", lambda e, i_=sps[tb][0:8, 0:512],
                  o=sexp_sb[0:8, tb * 512:(tb + 1) * 512]:
                  e.activation(out=o, in_=i_, func=AF.Exp),
                  wait=[t_s], inc="act")
        t_es = mm(esum[tb][0:1, 0:512], ones8_f[:, :],
                  sexp_sb[0:8, tb * 512:(tb + 1) * 512], True, True,
                  wait=[t_exp], inc="pe")
        t_we = mm(wexp[tb][:, :], prep_sb[:, :],
                  sexp_sb[0:8, tb * 512:(tb + 1) * 512], True, True,
                  wait=[t_exp], inc="pe")
        t_rc = E("dve", lambda e, i_=esum[tb][0:1, 0:512],
                 o=rsum_sb[0:1, tb * 512:(tb + 1) * 512]:
                 e.reciprocal(o, i_), wait=[t_es], inc="dve")
        t_rb = mm(sps[tb][:, :], onesr_f[:, :],
                  rsum_sb[0:1, tb * 512:(tb + 1) * 512], True, True,
                  wait=[t_rc], inc="pe")
        t_tmp = E("dve", lambda e, h=hcol[:, tb * 512:(tb + 1) * 512],
                  w_=wexp[tb], o=rb_sb[:, tb * 512:(tb + 1) * 512]:
                  e.tensor_tensor(o, h, w_[:, :], ALU.mult),
                  wait=[t_we, ev[(4, tb)]], inc="dve")
        E("dve", lambda e, t_=rb_sb[:, tb * 512:(tb + 1) * 512],
          r_=sps[tb], o=hw_sb[:, tb * 512:(tb + 1) * 512]:
          e.tensor_tensor(o, t_, r_[:, :], ALU.mult),
          wait=[t_rb, t_tmp], inc="dve")
    ps.close()

    ps = contextlib.ExitStack()
    scp = [ps.enter_context(nc.psum_tensor("scp%d" % i, [128, 512], F32))
           for i in range(4)]
    avp = [ps.enter_context(nc.psum_tensor("avp%d" % i, [128, 512], F32))
           for i in range(2)]
    dnp = [ps.enter_context(nc.psum_tensor("dnp%d" % i, [128, 512], F32))
           for i in range(2)]

    sc_free = [None] * 4
    probs_free = [()] * 4
    av_free = [None, None]
    dn_free = [None, None]
    rrow_free = [None, None]
    rb_free = [None, None]
    of_free = [None, None]

    def cc_fn(e, half):
        with allow_ici_all_to_all():
            return e.collective_compute(
                "AllToAll", ALU.bypass,
                replica_groups=[[0, 1, 2, 3, 4, 5, 6, 7]],
                ins=[att_send[half].opt()], outs=[att_recv[half].opt()])
    t_tri = dma("pool", strip_bf[:, 5120:6144], masks_in)
    wg = [stream_w[:, 0:4096].rearrange("p (k c) -> p k c", c=512),
          stream_w[:, 4096:8192].rearrange("p (k c) -> p k c", c=512)]
    pre_tg = {}
    for g0 in range(2):
        pre_tg[(0, g0)] = dma("pool", wg[g0 % 2],
                              wt_in[g0 * 1024:(g0 + 1) * 1024, 0:512]
                              .rearrange("(k p) c -> p k c", p=128),
                              wait=[w_free[g0 % 2]])
    pre_bm = dma("pool", bm_tile[:, 0:512], bm_in[:, 0:512])
    pre_br = dma("pool", brow_sb[0:1, 0:512], brow_in[0:1, 0:512])
    it = 0
    jcnt = [0]
    first_exp_extra = list(strip_free[0]) + list(strip_free[1])
    pending = []
    pending_av = []
    tail_j = [0]

    def emit_tail(pt):
        pq, t_cpd, col0 = pt
        t_rbm = mm(dnp[pq][:, :], onesr_f[:, :],
                   rrow[0:1, pq * 512:(pq + 1) * 512], True, True,
                   wait=[t_cpd], inc="pe")
        rrow_free[pq] = t_rbm
        t_rcp = E("dve", lambda e, i_=dnp[pq][:, :],
                  o=rb_sb[:, pq * 512:(pq + 1) * 512]:
                  e.reciprocal(o, i_),
                  wait=[t_rbm, rb_free[pq]], inc="dve")
        dn_free[pq] = t_rcp
        t_ev = E("dve", lambda e, a=avp[pq][:, :],
                 r_=rb_sb[:, pq * 512:(pq + 1) * 512],
                 o=big_a[:, col0:col0 + 512]:
                 e.tensor_tensor(o, a, r_, ALU.mult),
                 wait=[t_rcp], inc="dve")
        av_free[pq] = t_ev
        rb_free[pq] = t_ev
        return t_ev

    def emit_av(pv):
        j_, kb_, texp_, pq_, nkb_ = pv
        pr = probs[:, j_ * 512:(j_ + 1) * 512]
        mm(avp[pq_][:, :], vtm[:, kb_ * 128:(kb_ + 1) * 128], pr,
           kb_ == 0, kb_ == nkb_ - 1,
           wait=[texp_, av_free[pq_] if kb_ == 0 else None])
        td = mm(dnp[pq_][:, :], onesc_bf[:, :], pr,
                kb_ == 0, kb_ == nkb_ - 1,
                wait=[dn_free[pq_] if kb_ == 0 else None], inc="pe")
        probs_free[j_] = (td,)
        return td

    for slot in range(8):
        qcol0 = slot * 4096
        slot_evac = None
        for g in range(8):
            gb = 512 * g
            nkb = 4 * (g + 1)
            pq = it % 2
            if slot < 3:
                t_of = dma("sp", offr3_sb[0:3, pq * 512:(pq + 1) * 512],
                           offr3_in[(slot * 8 + g) * 3:
                                    (slot * 8 + g + 1) * 3, :],
                           wait=[of_free[pq]])
            t_dn = None
            for kb in range(nkb):
                j = jcnt[0] % 4
                jcnt[0] += 1
                diag = kb >= gb // 128
                stab = slot < 3
                wv = [sc_free[j]]
                if kb == 0:
                    wv.append(t_tri)
                    if stab:
                        wv.append(t_of)
                last = not diag and not stab
                tsc = mm(scp[j][:, :],
                         kT[:, kb * 128:(kb + 1) * 128],
                         big_a[:, qcol0 + gb:qcol0 + gb + 512],
                         True, last, wait=wv,
                         inc="pe" if last else None)
                if diag:
                    moff = 384 - 128 * (kb - gb // 128)
                    tsc = mm(scp[j][:, :],
                             strip_bf[:, 6016:6144],
                             strip_bf[:, 5120 + moff:5120 + moff + 512],
                             False, not stab,
                             inc=None if stab else "pe")
                if stab:
                    tsc = mm(scp[j][:, :], ones3[0:3, :],
                             offr3_sb[0:3, pq * 512:(pq + 1) * 512],
                             False, True, inc="pe")
                if len(pending_av) >= 2:
                    emit_av(pending_av.pop(0))
                if slot < 3:
                    c = BCOLS[(slot, "k", kb)]
                else:
                    c = BCOLS[(slot, (g + 1) * 512 - 128 * kb)]
                texp = E("act", lambda e, i_=scp[j][:, :],
                         o=probs[:, j * 512:(j + 1) * 512],
                         b=btab_sb[:, c:c + 1]:
                         e.activation(out=o, in_=i_, func=AF.Exp,
                                      scale=QK_SCALE, bias=b),
                         wait=[tsc, t_btab] + list(probs_free[j])
                         + first_exp_extra,
                         inc="act")
                sc_free[j] = texp
                if kb == 1 and pending:
                    emit_tail(pending.pop(0))
                pending_av.append((j, kb, texp, pq, nkb))
            while pending_av:
                t_dn = emit_av(pending_av.pop(0))
            if slot < 3:
                of_free[pq] = tsc
            t_cpd = E("act", lambda e, i_=dnp[pq][0:1, :],
                      o=rrow[0:1, pq * 512:(pq + 1) * 512]:
                      e.copy(o, i_),
                      wait=[t_dn, rrow_free[pq]], inc="act")
            dn_free[pq] = t_cpd
            pending.append((pq, t_cpd, qcol0 + gb))
            it += 1
        while pending:
            slot_evac = emit_tail(pending.pop(0))
        half = slot // 4
        pr_ = slot % 4
        for mb in range(8):
            m = mb % 4
            dma("sp", att_send[half][mb, pr_ * 128:(pr_ + 1) * 128, :],
                big_a[:, slot * 4096 + m * 1024:
                      slot * 4096 + (m + 1) * 1024],
                wait=[slot_evac])
        if slot == 3:
            t_cca = E("pool", lambda e: cc_fn(e, 0),
                      wait=[("d_sp%d" % l, rec.cnt["d_sp%d" % l])
                            for l in range(N_LANES)], inc="cc")

    t_ccb = E("pool", lambda e: cc_fn(e, 1),
              wait=[("d_sp%d" % l, rec.cnt["d_sp%d" % l])
                    for l in range(N_LANES)], inc="cc")
    rec.flush("attn")
    if stop_after == "attn":
        return nc
    ps.close()

    ps = contextlib.ExitStack()
    po = [ps.enter_context(nc.psum_tensor("po%d" % i, [128, 512], F32))
          for i in range(8)]

    attn_lds = []

    def _grp_reg(e):
        if "grp" not in rec.rtv:
            rec.rtv["grp"] = (e.partition_id() // 4) % 2
        return rec.rtv["grp"]

    for s_l in range(4):
        for half in range(2):
            def ld_fn(e, s_l=s_l, half=half):
                g2 = _grp_reg(e)
                r2 = att_recv[half].rearrange("m r t -> (m r) t")
                blk_ = r2[bass.ts(g2, 2048), :][s_l * 512:(s_l + 1) * 512, :]
                return e.dma_start(
                    out=big_a[:, s_l * 8192 + half * 4096:
                              s_l * 8192 + (half + 1) * 4096]
                    .rearrange("p (c t) -> p c t", t=1024),
                    in_=blk_.rearrange("(c p) t -> p c t", p=128))
            lane, pre = _lane("sp")
            t_attn = E("sp", ld_fn,
                       wait=[("cc", 1 if half == 0 else 2), pre],
                       inc=lane, step=16)
            attn_lds.append(t_attn)

    def lhs_for(g, k, tc):
        if g >= 4:
            i = (g - 4) * 8 + k
            half = i // 16
            r = i % 16
            c0 = (r // 4) * 8192 + half * 4096 + (r % 4) * 1024 + tc * 128
            return big_a[:, c0:c0 + 128]
        ic = g * 8 + k
        c0 = (tc // 4) * 16384 + ic * 512 + (tc % 4) * 128
        return big_b[:, c0:c0 + 128]

    grp_free = [None, None]
    po_free = [slot_evac] * 8
    x_free = [None] * 8
    out_free = [None, None]
    bm_free = None
    for ob in range(8):
        obp = ob % 2
        xl = []
        for tc in range(8):
            xl.append(dma(
                "sp", strip0[:, tc * 512:(tc + 1) * 512],
                xown_in[tc * 128:(tc + 1) * 128, ob * 512:(ob + 1) * 512],
                wait=[x_free[tc]]))
        if ob == 0:
            t_bm, t_br = pre_bm, pre_br
        else:
            t_bm = dma("pool", bm_tile[:, 0:512],
                       bm_in[:, ob * 512:(ob + 1) * 512], wait=[bm_free])
            t_br = dma("pool", brow_sb[0:1, 0:512],
                       brow_in[0:1, ob * 512:(ob + 1) * 512], wait=[bm_free])
        for g in range(8):
            sw = g % 2
            if g >= 4:
                src = wo_in[(g - 4) * 1024:(g - 3) * 1024,
                            ob * 512:(ob + 1) * 512]
            else:
                src = wt_in[g * 1024:(g + 1) * 1024,
                            ob * 512:(ob + 1) * 512]
            if (ob, g) in pre_tg:
                tg = pre_tg[(ob, g)]
            else:
                tg = dma("pool", wg[sw],
                         src.rearrange("(k p) c -> p k c", p=128),
                         wait=[grp_free[sw]])
            if g < 7:
                last = None
                for tc in range(8):
                    for k in range(8):
                        wv = []
                        if tc == 0 and k == 0:
                            wv.append(tg)
                            if ob == 0 and g == 4:
                                wv.extend(attn_lds[0::2])
                            if ob == 0 and g == 6:
                                wv.extend(attn_lds[1::2])
                        if g == 0 and k == 0:
                            wv.append(po_free[tc])
                        last = mm(po[tc][:, :], lhs_for(g, k, tc),
                                  wg[sw][:, k, :],
                                  g == 0 and k == 0, False, wait=wv,
                                  inc="pe" if (tc == 7 and k == 7) else None)
                grp_free[sw] = last
            else:
                t_last_tail = None
                for tc in range(8):
                    for k in range(8):
                        wv = []
                        if tc == 0 and k == 0:
                            wv.append(tg)
                        mm(po[tc][:, :], lhs_for(7, k, tc), wg[sw][:, k, :],
                           False, False, wait=wv)
                    mm(po[tc][:, :], hw_sb[:, tc * 128:(tc + 1) * 128],
                       bm_tile[:, 0:512], False, False,
                       wait=[t_bm])
                    t_tail = mm(po[tc][:, :], onesr_bf[:, :],
                                brow_sb[0:1, 0:512],
                                False, True,
                                wait=[t_br], inc="pe")
                    t_last_tail = t_tail
                    oc = (ob * 8 + tc) % 2
                    t_ev = E("dve", lambda e, a=po[tc],
                             x_=strip0[:, tc * 512:(tc + 1) * 512],
                             o=out_sb[:, oc * 512:(oc + 1) * 512]:
                             e.tensor_tensor(o, a[:, :], x_, ALU.add),
                             wait=[t_tail, xl[tc], out_free[oc]],
                             inc="dve")
                    po_free[tc] = t_ev
                    x_free[tc] = t_ev
                    t_st = dma("act",
                               out_dram[tc * 128:(tc + 1) * 128,
                                        ob * 512:(ob + 1) * 512],
                               out_sb[:, oc * 512:(oc + 1) * 512],
                               wait=[t_ev])
                    out_free[oc] = t_st
                grp_free[sw] = t_last_tail
        bm_free = t_last_tail
    E("sp", None, wait=[("%s%d" % (q, l), rec.cnt["%s%d" % (q, l)])
                        for q in ("d_act", "d_pool", "d_sp")
                        for l in range(N_LANES)])
    rec.flush("epilogue")
    ps.close()
    return nc


_PROG = None


def _consts():
    masks = np.zeros((128, 1024), dtype=np.float32)
    c = np.arange(128)
    t = np.arange(896)
    band = (c[:, None] == t[None, :] - 383).astype(np.float32)
    band[0, :384] = 1.0
    masks[:, :896] = band * -1e9
    k = np.arange(128)
    masks[:, 896:1024] = (k[None, :] >= c[:, None]).astype(np.float32)
    prep = np.zeros((8, 128), dtype=np.float32)
    for e in range(8):
        prep[e, e * 16:(e + 1) * 16] = 1.0
    bdiag = np.zeros((128, 16), dtype=np.float32)
    p_arr = np.arange(128)
    for e in range(4):
        bdiag[p_arr // 32 == e, e] = ISC32
    for e in range(4, 8):
        bdiag[p_arr // 32 == e - 4, 8 + e] = ISC32
    ident = np.eye(128, dtype=np.float32)
    return masks, prep, bdiag, ident


def _split3(v):
    h1 = v.astype(BF)
    r = v - h1.astype(np.float64)
    h2 = r.astype(BF)
    h3 = (r - h2.astype(np.float64)).astype(BF)
    return h1, h2, h3


def _bias_table(g):
    bt = np.zeros((128, NBC), dtype=np.float32)
    p = np.arange(128, dtype=np.float64)
    for key, c in BCOLS.items():
        s = key[0]
        sl = SLOPES[4 * s + g]
        if len(key) == 2:
            delta = min(sl * (NQ_SLOT[s] - 1) / 2.0, DELTA_CAP)
            bt[:, c] = (sl * (p - key[1]) + delta).astype(np.float32)
        else:
            bt[:, c] = (sl * (128 * key[2] + p)).astype(np.float32)
    return bt


def _offr3_table(g):
    t = np.zeros((24, 3, 512), dtype=BF)
    for s in range(3):
        sl = SLOPES[4 * s + g]
        nq = NQ_SLOT[s]
        delta = min(sl * (nq - 1) / 2.0, DELTA_CAP)
        for grp in range(8):
            qabs = grp * 512 + np.arange(512, dtype=np.float64)
            qhi = (np.floor(qabs / nq) + 1.0) * nq
            off = (delta - sl * qhi) / QK_SCALE
            o1, o2, o3 = _split3(off)
            t[s * 8 + grp, 0], t[s * 8 + grp, 1], t[s * 8 + grp, 2] = \
                o1, o2, o3
    return np.ascontiguousarray(t.reshape(72, 512))


def kernel(L0_attn_hidden_states, L0_attn_Wqkv, L0_attn_Wo, L0_ffn_W,
           L0_ffn_b, L0_ffn_Wq, L0_ffn_Wk, L0_ffn_A, L0_ffn_Bm):
    global _PROG
    x = np.ascontiguousarray(np.asarray(L0_attn_hidden_states,
                                        dtype=np.float32))
    wqkv = np.ascontiguousarray(np.asarray(L0_attn_Wqkv, dtype=np.float32))
    wo = np.asarray(L0_attn_Wo, dtype=np.float32)
    w = np.asarray(L0_ffn_W, dtype=np.float32)
    bv = np.asarray(L0_ffn_b, dtype=np.float32)
    wq = np.asarray(L0_ffn_Wq, dtype=np.float32)
    wk = np.asarray(L0_ffn_Wk, dtype=np.float32)
    a_ = np.asarray(L0_ffn_A, dtype=np.float32)
    bm = np.asarray(L0_ffn_Bm, dtype=np.float32)

    perm = np.concatenate([np.arange(128) + (4 * (hf * 4 + cc_) + sl) * 128
                           for hf in range(2) for sl in range(4)
                           for cc_ in range(4)])
    wo_p = np.ascontiguousarray(wo[perm, :])
    wt = np.ascontiguousarray(w.T)
    wqka = np.ascontiguousarray(
        np.concatenate([wq, wk, a_.reshape(128, DM)], axis=0).T)
    bm_f = np.ascontiguousarray(
        bm.transpose(0, 2, 1).reshape(128, DM))
    brow = np.ascontiguousarray(bv.reshape(1, DM))
    masks, prep, bdiag, ident = _consts()

    in_maps = []
    for c in range(8):
        b, g = c // 4, c % 4
        btab = _bias_table(g)
        offr3 = _offr3_table(g)
        wq8 = np.ascontiguousarray(np.concatenate(
            [wqkv[:, (4 * p + g) * 128:(4 * p + g + 1) * 128]
             for p in range(8)], axis=1))
        in_maps.append({
            "x": x[b],
            "x_own": np.ascontiguousarray(x[b][1024 * g:1024 * (g + 1)]),
            "wq8": wq8,
            "wkv": np.ascontiguousarray(wqkv[:, 4096:4352]),
            "wo": wo_p,
            "wt": wt,
            "brow": brow,
            "wqka": wqka,
            "bm": bm_f,
            "prep": prep,
            "bdiag": bdiag,
            "masks": masks,
            "btab": btab,
            "offr3": offr3,
            "ident": ident,
        })
    if _PROG is None:
        _PROG = build_program()
    res = run_bass_kernel_spmd(_PROG, in_maps, core_ids=list(range(8)))
    out = np.empty((2, S, DM), dtype=np.float32)
    for c in range(8):
        b, g = c // 4, c % 4
        out[b, 1024 * g:1024 * (g + 1), :] = np.asarray(
            res.results[c]["out"])
    return out
